# version 11
# baseline (speedup 1.0000x reference)
import numpy as np
from contextlib import ExitStack
import concourse.bass as bass
import concourse.mybir as mybir

F32 = mybir.dt.float32
BF16 = mybir.dt.bfloat16
FP8 = mybir.dt.float8e4
U16 = mybir.dt.uint16
U32 = mybir.dt.uint32
AF = mybir.ActivationFunctionType
ALU = mybir.AluOpType
AX = mybir.AxisListType

A4 = 1.0 + 0.95 + 0.95**2 + 0.95**3
TH = [float(np.float32(1.0 / A4)), float(np.float32(1.95 / A4)),
      float(np.float32(2.95 / A4)), float(np.float32(3.8525 / A4))]
MAGIC = float(np.float32(1.5 * 2**23))
SIXTH = float(np.float32(1.0 / 6.0))

D = 1024
I = 2816
NDC = 8
NIC = 22
NP = 8

ENGINES = ("sp", "pe", "dve", "act", "pool")
SETTLE_OPS = 4
ALLSEMS = ("spq", "actq", "poolq", "pe", "dve", "act", "pool")


class Sched:
    def __init__(self):
        self.ops = {e: [] for e in ENGINES}
        self.cnt = {}
        self.lastwait = {}

    def n(self, sem):
        return self.cnt.get(sem, 0)

    SELF_SYNC = {"dve": "dve", "act": "act", "pool": "pool"}

    def emit(self, eng, fn, sem=None, inc=0, waits=()):
        w = []
        ss = self.SELF_SYNC.get(eng)
        if ss is not None and self.n(ss) > 0:
            waits = list(waits) + [(ss, self.n(ss))]
        for (s, v) in waits:
            if v is None or v <= 0:
                continue
            key = (eng, s)
            if self.lastwait.get(key, -1) >= v:
                continue
            self.lastwait[key] = v
            w.append((s, v))
        self.ops[eng].append((fn, sem, inc, tuple(w)))
        if sem is not None:
            self.cnt[sem] = self.cnt.get(sem, 0) + inc

    def run(self, block, sems):
        secmap = {"sp": block.sync, "pe": block.tensor, "dve": block.vector,
                  "act": block.scalar, "pool": block.gpsimd}
        for ename in ENGINES:
            oplist = self.ops[ename]
            if not oplist:
                continue

            def body(eng, oplist=oplist):
                for (fn, sem, inc, waits) in oplist:
                    for (s, v) in waits:
                        eng.wait_ge(sems[s], v)
                    ins = fn(eng)
                    if sem is not None:
                        ins.then_inc(sems[sem], inc)
            secmap[ename](body)
        self.ops = {e: [] for e in ENGINES}


def settle(sch, dst, src):
    for _ in range(SETTLE_OPS):
        sch.emit("dve", lambda e: e.tensor_copy(dst, src), "dve", 1)


def quant256(sch, src_ap, dst_ap, src_waits, dst_waits, ts):
    W = src_ap.shape[1]
    nb = W // 16
    w3 = src_ap.rearrange("p (b c) -> p b c", c=16)
    sw = list(src_waits)
    sch.emit("dve", lambda e: e.tensor_reduce(
        ts["blk"][:, 0:nb], w3, AX.X, ALU.max, apply_absolute_value=True),
        "dve", 1, sw + [("dve", ts["busy_dve"])])
    sch.emit("dve", lambda e: e.tensor_scalar(
        ts["scl"][:, 0:nb], ts["blk"][:, 0:nb], SIXTH, 1e-12, ALU.mult, ALU.max),
        "dve", 1)
    sch.emit("dve", lambda e: e.reciprocal(ts["rcp"][:, 0:nb], ts["scl"][:, 0:nb]),
             "dve", 1)
    sch.emit("dve", lambda e: e.tensor_scalar(
        ts["r2"][:, 0:nb], ts["rcp"][:, 0:nb], 2.0, None, ALU.mult), "dve", 1)
    r2b = ts["r2"][:, 0:nb].unsqueeze(2).broadcast_to((128, nb, 16))
    sch.emit("dve", lambda e: e.tensor_tensor(
        ts["u2"][:, 0:W].rearrange("p (b c) -> p b c", c=16), w3, r2b, ALU.mult),
        "dve", 1, sw)
    sch.emit("dve", lambda e: e.tensor_scalar(
        ts["a2"][:, 0:W].bitcast(U32), ts["u2"][:, 0:W].bitcast(U32),
        0x7FFFFFFF, None, ALU.bitwise_and), "dve", 1)
    sch.emit("dve", lambda e: e.tensor_scalar(
        ts["c1"][:, 0:W], ts["a2"][:, 0:W], 5.0, None, ALU.is_gt), "dve", 1)
    sch.emit("dve", lambda e: e.tensor_scalar(
        ts["c2"][:, 0:W], ts["a2"][:, 0:W], 7.0, None, ALU.is_gt), "dve", 1)
    sch.emit("dve", lambda e: e.tensor_scalar(
        ts["c3"][:, 0:W], ts["a2"][:, 0:W], 10.0, 2.0, ALU.is_gt, ALU.mult),
        "dve", 1)
    sch.emit("dve", lambda e: e.tensor_scalar(
        ts["a2"][:, 0:W], ts["u2"][:, 0:W], MAGIC, MAGIC, ALU.add, ALU.subtract),
        "dve", 1)
    sch.emit("dve", lambda e: e.tensor_scalar(
        ts["u2"][:, 0:W], ts["a2"][:, 0:W], 4.0, None, ALU.min), "dve", 1)
    sch.emit("dve", lambda e: e.tensor_scalar(
        ts["q0"][:, 0:W], ts["u2"][:, 0:W], -4.0, None, ALU.max), "dve", 1)
    sch.emit("dve", lambda e: e.tensor_tensor(
        ts["sA"][:, 0:W], ts["c1"][:, 0:W], ts["c2"][:, 0:W], ALU.add),
        "dve", 1)
    sch.emit("dve", lambda e: e.tensor_tensor(
        ts["c2"][:, 0:W], ts["sA"][:, 0:W], ts["c3"][:, 0:W], ALU.add), "dve", 1)
    sch.emit("dve", lambda e: e.tensor_scalar(
        ts["sA"][:, 0:W].bitcast(U16), ts["q0"][:, 0:W].bitcast(U16), 0x8000, None,
        ALU.bitwise_and), "dve", 1)
    sch.emit("dve", lambda e: e.tensor_tensor(
        ts["c1"][:, 0:W].bitcast(U16), ts["c2"][:, 0:W].bitcast(U16),
        ts["sA"][:, 0:W].bitcast(U16), ALU.bitwise_or), "dve", 1)
    sch.emit("dve", lambda e: e.scalar_tensor_tensor(
        ts["sA"][:, 0:W], ts["q0"][:, 0:W], 0.5, ts["c1"][:, 0:W],
        ALU.mult, ALU.add), "dve", 1)
    sclb = ts["scl"][:, 0:nb].unsqueeze(2).broadcast_to((128, nb, 16))
    sch.emit("dve", lambda e: e.tensor_tensor(
        dst_ap.rearrange("p (b c) -> p b c", c=16),
        ts["sA"][:, 0:W].rearrange("p (b c) -> p b c", c=16), sclb, ALU.mult),
        "dve", 1, list(dst_waits))
    ts["busy_dve"] = sch.n("dve")
    return None


def build_program(S=4096, dbg=False):
    ST = S // 128
    NQ = S // 512
    NK = S // 128
    nc = bass.Bass("TRN2", target_bir_lowering=False)

    x_h = nc.dram_tensor("x", [S, D], F32, kind="ExternalInput").ap()
    qw_h = nc.dram_tensor("qw", [D, D], F32, kind="ExternalInput").ap()
    qb_h = nc.dram_tensor("qb", [D], F32, kind="ExternalInput").ap()
    kw_h = nc.dram_tensor("kw", [D, D], F32, kind="ExternalInput").ap()
    kb_h = nc.dram_tensor("kb", [D], F32, kind="ExternalInput").ap()
    vw_h = nc.dram_tensor("vw", [D, D], F32, kind="ExternalInput").ap()
    vb_h = nc.dram_tensor("vb", [D], F32, kind="ExternalInput").ap()
    ow_h = nc.dram_tensor("ow", [D, D], F32, kind="ExternalInput").ap()
    ob_h = nc.dram_tensor("ob", [D], F32, kind="ExternalInput").ap()
    wg_h = nc.dram_tensor("wg", [I, D], F32, kind="ExternalInput").ap()
    wu_h = nc.dram_tensor("wu", [I, D], F32, kind="ExternalInput").ap()
    wd_h = nc.dram_tensor("wd", [D, I], F32, kind="ExternalInput").ap()
    out_h = nc.dram_tensor("out", [S, D], F32, kind="ExternalOutput").ap()
    ident_h = nc.dram_tensor("ident", [128, 128], BF16, kind="ExternalInput").ap()

    kw = dict(kind="ExternalOutput") if dbg else {}
    xq_hbm = nc.dram_tensor("xq_hbm", [S, D], BF16, **kw).ap()
    wgq_hbm = nc.dram_tensor("wgq_hbm", [I, D], BF16, **kw).ap()
    wuq_hbm = nc.dram_tensor("wuq_hbm", [I, D], BF16, **kw).ap()
    wdq_hbm = nc.dram_tensor("wdq_hbm", [D, I], BF16, **kw).ap()
    aot_hbm = nc.dram_tensor("aot_hbm", [D, S], BF16, **kw).ap()
    ffn_hbm = nc.dram_tensor("ffn_hbm", [S, D], BF16, **kw).ap()
    hq_hbm = nc.dram_tensor("hq_hbm", [S, I], BF16, **kw).ap() if dbg else None

    sch = Sched()

    def mkquanttemps(en, pfx):
        return [dict(
            blk=en(nc.sbuf_tensor(f"{pfx}blk{j}", [128, 32], F32)),
            scl=en(nc.sbuf_tensor(f"{pfx}scl{j}", [128, 32], F32)),
            rcp=en(nc.sbuf_tensor(f"{pfx}rcp{j}", [128, 32], F32)),
            r2=en(nc.sbuf_tensor(f"{pfx}r2{j}", [128, 32], F32)),
            u2=en(nc.sbuf_tensor(f"{pfx}u2{j}", [128, 512], F32)),
            a2=en(nc.sbuf_tensor(f"{pfx}a2{j}", [128, 512], F32)),
            q0=en(nc.sbuf_tensor(f"{pfx}q0{j}", [128, 512], BF16)),
            c1=en(nc.sbuf_tensor(f"{pfx}c1{j}", [128, 512], BF16)),
            c2=en(nc.sbuf_tensor(f"{pfx}c2{j}", [128, 512], BF16)),
            c3=en(nc.sbuf_tensor(f"{pfx}c3{j}", [128, 512], BF16)),
            sA=en(nc.sbuf_tensor(f"{pfx}sA{j}", [128, 512], BF16)),
            busy_dve=0,
        ) for j in range(2)]


    with ExitStack() as top:
        sems = {}
        for s in ALLSEMS:
            sems[s] = top.enter_context(nc.semaphore("sem_" + s))

        ones_l = top.enter_context(nc.sbuf_tensor("ones_l", [128, 512], F32))
        fours = top.enter_context(nc.sbuf_tensor("fours", [128, 1], BF16))
        biasrows = top.enter_context(nc.sbuf_tensor("biasrows", [65, D], F32))
        identb = top.enter_context(nc.sbuf_tensor("identb", [128, 128], BF16))

        sch.emit("sp", lambda e: e.dma_start(identb[:], ident_h), "spq", 16)
        sch.emit("pool", lambda e: e.memset(ones_l[:], 1.0), "pool", 1)
        sch.emit("pool", lambda e: e.memset(fours[:], 4.0), "pool", 1)
        const_pool = sch.n("pool")
        sch.emit("sp", lambda e: e.dma_start(biasrows[0:1, :], qb_h.unsqueeze(0)), "spq", 16)
        sch.emit("sp", lambda e: e.dma_start(biasrows[32:33, :], kb_h.unsqueeze(0)), "spq", 16)
        sch.emit("sp", lambda e: e.dma_start(biasrows[64:65, :], vb_h.unsqueeze(0)), "spq", 16)
        bias_ld = sch.n("spq")

        with ExitStack() as sa:
            en = sa.enter_context
            xnT = [en(nc.sbuf_tensor(f"xnT{d}", [128, S], BF16)) for d in range(NDC)]
            v8 = [en(nc.sbuf_tensor(f"v8_{t}", [128, D], FP8)) for t in range(NK)]
            q8 = [en(nc.sbuf_tensor(f"q8_{j}", [128, S], FP8)) for j in range(2)]
            k8 = [en(nc.sbuf_tensor(f"k8_{j}", [128, S], FP8)) for j in range(2)]
            vbf = [en(nc.sbuf_tensor(f"vbf{t}", [128, 128], BF16)) for t in range(NK)]
            attnT = [en(nc.sbuf_tensor(f"attnT{j}", [128, 1024], BF16)) for j in range(2)]
            x2 = [en(nc.sbuf_tensor(f"x2_{j}", [128, D], F32)) for j in range(2)]
            xn = en(nc.sbuf_tensor("xn", [128, D], F32))
            xnbf = [en(nc.sbuf_tensor(f"xnbf{j}", [128, D], BF16)) for j in range(2)]
            vw_bf = [en(nc.sbuf_tensor(f"vwbf{d}", [128, D], BF16)) for d in range(NDC)]
            qkw_st = [en(nc.sbuf_tensor(f"qkwst{j}", [128, 512], F32)) for j in range(2)]
            qw_bf = en(nc.sbuf_tensor("qw_bf", [128, D], BF16))
            kw_bf = en(nc.sbuf_tensor("kw_bf", [128, D], BF16))
            wstage = en(nc.sbuf_tensor("wstage", [128, 512], F32))
            wqstage = en(nc.sbuf_tensor("wqstage", [128, 512], BF16))
            xqstage = [en(nc.sbuf_tensor(f"xqst{j}", [128, D], BF16)) for j in range(2)]
            serec = en(nc.sbuf_tensor("serec", [65, 512], F32))
            rb_sb = en(nc.sbuf_tensor("rb_sb", [128, 512], F32))
            aotst = [en(nc.sbuf_tensor(f"aotst{j}", [128, 512], BF16)) for j in range(2)]
            lt = [en(nc.sbuf_tensor(f"lt{j}", [128, 512], BF16)) for j in range(2)]
            bn6 = en(nc.sbuf_tensor("bn6", [128, 12], F32))
            bn2 = en(nc.sbuf_tensor("bn2", [128, 2], F32))
            rstd = en(nc.sbuf_tensor("rstd", [128, 1], F32))
            negmr = en(nc.sbuf_tensor("negmr", [128, 1], F32))
            settleA = en(nc.sbuf_tensor("settleA", [128, 64], F32))
            qt = mkquanttemps(en, "qa")
            sps = [en(nc.psum_tensor(f"sps{j}", [128, 1024], F32)) for j in range(2)]
            avps = en(nc.psum_tensor("avps", [128, 512], F32))
            seps = en(nc.psum_tensor("seps", [65, 512], F32))
            rbps = en(nc.psum_tensor("rbps", [128, 512], F32))
            pps = en(nc.psum_tensor("pps", [128, 512], F32))

            wplan = []
            for (srch, dsth, rows, cols) in ((wg_h, wgq_hbm, NIC, D), (wu_h, wuq_hbm, NIC, D)):
                for r in range(rows):
                    for c in range(cols // 512):
                        wplan.append((srch[r * 128:(r + 1) * 128, c * 512:(c + 1) * 512],
                                      dsth[r * 128:(r + 1) * 128, c * 512:(c + 1) * 512]))
            for r in range(NDC):
                for c in range(5):
                    wplan.append((wd_h[r * 128:(r + 1) * 128, c * 512:(c + 1) * 512],
                                  wdq_hbm[r * 128:(r + 1) * 128, c * 512:(c + 1) * 512]))
                wplan.append((wd_h[r * 128:(r + 1) * 128, I - 256:I],
                              wdq_hbm[r * 128:(r + 1) * 128, I - 256:I]))

            wstate = dict(i=0, ld=0, stage_free_dve=0, store_done=0, nq=0)

            def pump_wprep(nunits):
                for _ in range(nunits):
                    if wstate["i"] >= len(wplan):
                        return
                    i = wstate["i"]
                    src, dst = wplan[i]
                    width = src.shape[1]
                    sch.emit("sp", lambda e, src=src, width=width: e.dma_start(
                        wstage[:, 0:width], src), "spq", 16,
                        [("dve", wstate["stage_free_dve"])])
                    wstate["ld"] = sch.n("spq")
                    ts = qt[wstate["nq"] % 2]
                    wstate["nq"] += 1
                    quant256(sch, wstage[:, 0:width], wqstage[:, 0:width],
                             [("spq", wstate["ld"])],
                             [("actq", wstate["store_done"])], ts)
                    wstate["stage_free_dve"] = sch.n("dve")
                    settle(sch, settleA[:], wqstage[:, 0:64])
                    dvn = sch.n("dve")
                    sch.emit("act", lambda e, dst=dst, width=width: e.dma_start(
                        dst, wqstage[:, 0:width]), "actq", 16, [("dve", dvn)])
                    wstate["store_done"] = sch.n("actq")
                    wstate["i"] += 1

            vw_marks = []
            for u in range(16):
                dc, half = divmod(u, 2)
                j = u % 2
                wt = [("dve", vw_marks[u - 2])] if u >= 2 else []
                sch.emit("sp", lambda e, dc=dc, half=half, j=j: e.dma_start(
                    qkw_st[j][:], vw_h[dc * 128:(dc + 1) * 128, half * 512:(half + 1) * 512]),
                    "spq", 16, wt)
                ld = sch.n("spq")
                sch.emit("dve", lambda e, dc=dc, half=half, j=j: e.tensor_copy(
                    vw_bf[dc][:, half * 512:(half + 1) * 512], qkw_st[j][:]),
                    "dve", 1, [("spq", ld)])
                vw_marks.append(sch.n("dve"))
            vw_cast_done = sch.n("dve")

            x_rd = [0, 0]
            xnbf_free = [(0, 0), (0, 0)]
            xqst_free = [0, 0]
            xnT_done = []
            vlif_done = 0

            for st in range(ST):
                j = st % 2
                sch.emit("sp", lambda e, st=st, j=j: e.dma_start(
                    x2[j][:], x_h[st * 128:(st + 1) * 128, :]), "spq", 16,
                    [("dve", x_rd[j])])
                xld = sch.n("spq")
                sch.emit("dve", lambda e, j=j: e.bn_stats(bn6[:, 0:6], x2[j][:, 0:512]),
                         "dve", 1, [("spq", xld)])
                sch.emit("dve", lambda e, j=j: e.bn_stats(bn6[:, 6:12], x2[j][:, 512:1024]),
                         "dve", 1)
                sch.emit("dve", lambda e: e.bn_aggr(bn2[:], bn6[:]), "dve", 1)
                bnd = sch.n("dve")
                sch.emit("act", lambda e: e.activation(rstd[:], bn2[:, 1:2], AF.Ln, bias=np.float32(1e-5)),
                         "act", 1, [("dve", bnd)])
                sch.emit("act", lambda e: e.activation(rstd[:], rstd[:], AF.Exp, scale=np.float32(-0.5)),
                         "act", 1)
                rstd_done = sch.n("act")
                sch.emit("dve", lambda e: e.scalar_tensor_tensor(
                    negmr[:], bn2[:, 0:1], -1.0, rstd[:], ALU.mult, ALU.mult),
                    "dve", 1, [("act", rstd_done)])
                sch.emit("dve", lambda e, j=j: e.tensor_scalar(
                    xn[:], x2[j][:], rstd[:], negmr[:], ALU.mult, ALU.add), "dve", 1)
                sch.emit("dve", lambda e, j=j: e.tensor_copy(xnbf[j][:], xn[:]), "dve", 1,
                         [("spq", xnbf_free[j][0]), ("actq", xnbf_free[j][1])])
                xn_done = sch.n("dve")
                x_rd[j] = xn_done
                for dc in range(NDC):
                    qeng, qsem = (("sp", "spq") if dc % 2 == 0 else ("act", "actq"))
                    sch.emit(qeng, lambda e, dc=dc, st=st, j=j: e.dma_start_transpose(
                        xnT[dc][:, st * 128:(st + 1) * 128],
                        xnbf[j][:, dc * 128:(dc + 1) * 128]), qsem, 16,
                        [("dve", xn_done)])
                xnbf_free[j] = (sch.n("spq"), sch.n("actq"))
                xnT_done.append((sch.n("spq"), sch.n("actq")))
                for c in range(2):
                    ts = qt[c % 2]
                    quant256(sch,
                        xn[:, c * 512:(c + 1) * 512], xqstage[j][:, c * 512:(c + 1) * 512],
                        [("dve", xn_done)], [("actq", xqst_free[j])], ts)
                settle(sch, settleA[:], xqstage[j][:, 0:64])
                dvn = sch.n("dve")
                sch.emit("act", lambda e, st=st, j=j: e.dma_start(
                    xq_hbm[st * 128:(st + 1) * 128, :], xqstage[j][:]), "actq", 16,
                    [("dve", dvn)])
                xqst_free[j] = sch.n("actq")
                tw = xnT_done[st]
                vwaits = [("spq", max(tw[0], bias_ld)), ("actq", tw[1]),
                          ("dve", max(vw_cast_done, vlif_done))]
                for half in range(2):
                    sch.emit("pe", lambda e, half=half: e.matmul(
                        sps[0][:, half * 512:(half + 1) * 512], ones_l[64:65, 0:128],
                        biasrows[64:65, half * 512:(half + 1) * 512],
                        start=True, stop=False, tile_position=(64, 0)), "pe", 1,
                        vwaits if half == 0 else ())
                for dc in range(NDC):
                    for half in range(2):
                        sch.emit("pe", lambda e, dc=dc, half=half, st=st: e.matmul(
                            sps[0][:, half * 512:(half + 1) * 512],
                            xnT[dc][:, st * 128:(st + 1) * 128],
                            vw_bf[dc][:, half * 512:(half + 1) * 512],
                            start=False, stop=(dc == NDC - 1)), "pe", 1)
                vproj = sch.n("pe")
                for half in range(2):
                    pv = sps[0][:, half * 512:(half + 1) * 512]
                    sch.emit("dve", lambda e, pv=pv: e.tensor_scalar(
                        lt[0][:], pv, TH[0], None, ALU.is_gt), "dve", 1, [("pe", vproj)])
                    sch.emit("dve", lambda e, pv=pv: e.scalar_tensor_tensor(
                        lt[1][:], pv, TH[1], lt[0][:], ALU.is_gt, ALU.add), "dve", 1)
                    sch.emit("dve", lambda e, pv=pv: e.scalar_tensor_tensor(
                        lt[0][:], pv, TH[2], lt[1][:], ALU.is_gt, ALU.add), "dve", 1)
                    sch.emit("dve", lambda e, pv=pv, st=st, half=half: e.scalar_tensor_tensor(
                        v8[st][:, half * 512:(half + 1) * 512], pv, TH[3], lt[0][:],
                        ALU.is_gt, ALU.add), "dve", 1)
                vlif_done = sch.n("dve")

            all_xnT = (sch.n("spq"), sch.n("actq"))

            projq_done = [0, 0]
            projk_done = [0, 0]
            q_freed = [0, 0]
            projlif_guard = [vlif_done]
            qkst_guard = [0]
            last_proj_pe = [0]

            def emit_proj_load(p):
                for (w_h, wbf, base) in ((qw_h, qw_bf, 0), (kw_h, kw_bf, 1)):
                    src = w_h[:, p * 128:(p + 1) * 128].rearrange("(c p) f -> p c f", p=128)
                    for half in range(2):
                        j = half
                        lo = half * 4
                        sch.emit("sp", lambda e, src=src, lo=lo, j=j: e.dma_start(
                            qkw_st[j][:].rearrange("p (c f) -> p c f", f=128),
                            src[:, lo:lo + 4, :]), "spq", 16, [("dve", qkst_guard[0])])
                        ld = sch.n("spq")
                        sch.emit("dve", lambda e, wbf=wbf, lo=lo, j=j: e.tensor_copy(
                            wbf[:, lo * 128:(lo + 4) * 128], qkw_st[j][:]), "dve", 1,
                            [("spq", ld), ("pe", last_proj_pe[0])])
                qkst_guard[0] = sch.n("dve")

            def emit_proj_unit(p, qc, which):
                par = p % 2
                isq = which == "q"
                wbf = qw_bf if isq else kw_bf
                dst = (q8 if isq else k8)[par]
                base = 0 if isq else 32
                tp = (0, 0) if base == 0 else (32, 0)
                waits = [("dve", max(projlif_guard[0], qkst_guard[0])),
                         ("spq", max(all_xnT[0], bias_ld)), ("actq", all_xnT[1])]
                sch.emit("pe", lambda e, base=base, p=p, tp=tp: e.matmul(
                    pps[:], biasrows[base:base + 1, p * 128:(p + 1) * 128],
                    ones_l[base:base + 1, 0:512],
                    start=True, stop=False, tile_position=tp), "pe", 1, waits)
                for dc in range(NDC):
                    sch.emit("pe", lambda e, dc=dc, qc=qc, wbf=wbf: e.matmul(
                        pps[:], wbf[:, dc * 128:(dc + 1) * 128],
                        xnT[dc][:, qc * 512:(qc + 1) * 512],
                        start=False, stop=(dc == NDC - 1)), "pe", 1)
                mm = sch.n("pe")
                last_proj_pe[0] = mm
                sch.emit("dve", lambda e: e.tensor_scalar(
                    lt[0][:], pps[:], TH[0], None, ALU.is_gt), "dve", 1, [("pe", mm)])
                sch.emit("dve", lambda e: e.scalar_tensor_tensor(
                    lt[1][:], pps[:], TH[1], lt[0][:], ALU.is_gt, ALU.add), "dve", 1)
                sch.emit("dve", lambda e: e.scalar_tensor_tensor(
                    lt[0][:], pps[:], TH[2], lt[1][:], ALU.is_gt, ALU.add), "dve", 1)
                sch.emit("dve", lambda e, dst=dst, qc=qc: e.scalar_tensor_tensor(
                    dst[:, qc * 512:(qc + 1) * 512], pps[:], TH[3], lt[0][:],
                    ALU.is_gt, ALU.add), "dve", 1)
                projlif_guard[0] = sch.n("dve")
                if isq:
                    projq_done[par] = sch.n("dve")
                else:
                    projk_done[par] = sch.n("dve")

            def emit_vbf(p):
                for t in range(NK):
                    sch.emit("dve", lambda e, t=t, p=p: e.tensor_copy(
                        vbf[t][:], v8[t][:, p * 128:(p + 1) * 128]), "dve", 1,
                        [("pe", vbf_guard[0])])
                return sch.n("dve")

            exp_done = [0, 0]
            attnT_used = [0, 0]
            epi_guard = [0]
            vbf_guard = [0]
            aot_free = [0, 0]

            emit_proj_load(0)
            for qc in range(NQ):
                emit_proj_unit(0, qc, "q")
                emit_proj_unit(0, qc, "k")
            vbf_ready = emit_vbf(0)

            for p in range(NP):
                par = p % 2
                nxt = ([(p + 1, qc, w) for qc in range(NQ) for w in ("q", "k")]
                       if p < NP - 1 else [])
                nxt_loaded = False
                for qc in range(NQ):
                    qwait = ("dve", max(projq_done[par], projk_done[par], vbf_ready))
                    for kt in range(NK):
                        jj = kt % 2
                        sch.emit("pe", lambda e, qc=qc, kt=kt, jj=jj, par=par: e.matmul(
                            sps[jj][:, 0:512], k8[par][0:64, kt * 128:(kt + 1) * 128],
                            q8[par][0:64, qc * 512:(qc + 1) * 512],
                            start=True, stop=True, tile_position=(0, 0)), "pe", 1,
                            [qwait, ("act", exp_done[jj])])
                        sch.emit("pe", lambda e, qc=qc, kt=kt, jj=jj, par=par: e.matmul(
                            sps[jj][:, 512:1024], k8[par][64:128, kt * 128:(kt + 1) * 128],
                            q8[par][64:128, qc * 512:(qc + 1) * 512],
                            start=True, stop=True, tile_position=(64, 0)), "pe", 1)
                        scdone = sch.n("pe")
                        sch.emit("act", lambda e, jj=jj: e.activation(
                            attnT[jj][:], sps[jj][:], AF.Exp, scale=np.float32(1.0 / 128.0)),
                            "act", 1, [("pe", scdone), ("pe", attnT_used[jj])])
                        exp_done[jj] = sch.n("act")
                        avw = [("act", exp_done[jj])]
                        if kt == 0:
                            avw.append(("dve", epi_guard[0]))
                        sch.emit("pe", lambda e, kt=kt, jj=jj: e.matmul(
                            avps[0:64, :], vbf[kt][:, 0:64], attnT[jj][:, 0:512],
                            start=(kt == 0), stop=(kt == NK - 1),
                            tile_position=(0, 0)), "pe", 1, avw)
                        sch.emit("pe", lambda e, kt=kt, jj=jj: e.matmul(
                            avps[64:128, :], vbf[kt][:, 64:128], attnT[jj][:, 512:1024],
                            start=(kt == 0), stop=(kt == NK - 1),
                            tile_position=(0, 64)), "pe", 1)
                        sew = ([("dve", epi_guard[0]), ("pool", const_pool)]
                               if kt == 0 else ())
                        sch.emit("pe", lambda e, kt=kt, jj=jj: e.matmul(
                            seps[0:1, :], fours[:], attnT[jj][:, 0:512],
                            start=(kt == 0), stop=(kt == NK - 1),
                            tile_position=(0, 0)), "pe", 1, sew)
                        sch.emit("pe", lambda e, kt=kt, jj=jj: e.matmul(
                            seps[32:33, :], fours[:], attnT[jj][:, 512:1024],
                            start=(kt == 0), stop=(kt == NK - 1),
                            tile_position=(0, 32)), "pe", 1)
                        attnT_used[jj] = sch.n("pe")
                    ktend = sch.n("pe")
                    sch.emit("dve", lambda e: e.reciprocal(serec[:], seps[:]), "dve", 1,
                             [("pe", ktend)])
                    rp = sch.n("dve")
                    sch.emit("pe", lambda e: e.matmul(
                        rbps[0:64, :], ones_l[0:1, 0:64], serec[0:1, :],
                        start=True, stop=True), "pe", 1, [("dve", rp)])
                    sch.emit("pe", lambda e: e.matmul(
                        rbps[64:128, :], ones_l[32:33, 0:64], serec[32:33, :],
                        start=True, stop=True, tile_position=(32, 64)), "pe", 1)
                    bc = sch.n("pe")
                    sch.emit("dve", lambda e: e.tensor_copy(rb_sb[:], rbps[:]), "dve", 1,
                             [("pe", bc)])
                    aj = qc % 2
                    sch.emit("dve", lambda e, aj=aj: e.tensor_tensor(
                        aotst[aj][:], avps[:], rb_sb[:], ALU.mult), "dve", 1,
                        [("actq", aot_free[aj])])
                    epi_guard[0] = sch.n("dve")
                    settle(sch, settleA[:], aotst[aj][:, 0:64])
                    aotd = sch.n("dve")
                    sch.emit("act", lambda e, p=p, qc=qc, aj=aj: e.dma_start(
                        aot_hbm[p * 128:(p + 1) * 128, qc * 512:(qc + 1) * 512],
                        aotst[aj][:]), "actq", 16, [("dve", aotd)])
                    aot_free[aj] = sch.n("actq")
                    if nxt and not nxt_loaded:
                        emit_proj_load(p + 1)
                        nxt_loaded = True
                    for _ in range(2):
                        if nxt:
                            pp, qq, ww = nxt.pop(0)
                            emit_proj_unit(pp, qq, ww)
                    pump_wprep(5)
                q_freed[par] = sch.n("pe")
                if p < NP - 1:
                    while nxt:
                        pp, qq, ww = nxt.pop(0)
                        emit_proj_unit(pp, qq, ww)
                    vbf_guard[0] = sch.n("pe")
                    vbf_ready = emit_vbf(p + 1)
            pump_wprep(10 ** 9)

            barrier = {s: sch.n(s) for s in ALLSEMS}
            with nc.Block() as blockA:
                sch.run(blockA, sems)

        with ExitStack() as sb:
            en = sb.enter_context
            WGQT = [en(nc.sbuf_tensor(f"WGQT{d}", [128, I], BF16)) for d in range(NDC)]
            WUQT = [en(nc.sbuf_tensor(f"WUQT{d}", [128, I], BF16)) for d in range(NDC)]
            WDQT = [en(nc.sbuf_tensor(f"WDQT{i}", [128, D], BF16)) for i in range(NIC)]
            xqt = [[en(nc.sbuf_tensor(f"xqt{j}_{d}", [128, 256], BF16))
                    for d in range(NDC)] for j in range(2)]
            sg = en(nc.sbuf_tensor("sg", [128, 512], F32))
            hbuf = en(nc.sbuf_tensor("hbuf", [128, 512], F32))
            hq2 = [en(nc.sbuf_tensor(f"hq2_{j}", [128, 512], BF16)) for j in range(2)]
            settleB = en(nc.sbuf_tensor("settleB", [128, 64], F32))
            hqT = [en(nc.sbuf_tensor(f"hqT{i}", [128, 256], BF16)) for i in range(NIC)]
            ffnst = [en(nc.sbuf_tensor(f"ffnst{j}", [128, D], BF16)) for j in range(2)]
            qtb = mkquanttemps(en, "qb")
            for t in qtb:
                t["busy_dve"] = barrier["dve"]
            gps = en(nc.psum_tensor("gps", [128, 512], F32))
            ups = en(nc.psum_tensor("ups", [128, 512], F32))
            dps = [en(nc.psum_tensor(f"dps{j}", [128, D], F32)) for j in range(2)]
            tpps = [en(nc.psum_tensor(f"tpps{j}", [128, 128], BF16)) for j in range(2)]

            bw = [(s, barrier[s]) for s in ALLSEMS]

            wgq_loaded = []
            for ic in range(NIC):
                for dc in range(NDC):
                    sch.emit("sp", lambda e, ic=ic, dc=dc: e.dma_start_transpose(
                        WGQT[dc][:, ic * 128:(ic + 1) * 128],
                        wgq_hbm[ic * 128:(ic + 1) * 128, dc * 128:(dc + 1) * 128]),
                        "spq", 16, bw)
                    sch.emit("act", lambda e, ic=ic, dc=dc: e.dma_start_transpose(
                        WUQT[dc][:, ic * 128:(ic + 1) * 128],
                        wuq_hbm[ic * 128:(ic + 1) * 128, dc * 128:(dc + 1) * 128]),
                        "actq", 16, bw)
                for dc in range(NDC):
                    qeng, qsem = (("sp", "spq") if dc % 2 == 0 else ("act", "actq"))
                    sch.emit(qeng, lambda e, ic=ic, dc=dc: e.dma_start_transpose(
                        WDQT[ic][:, dc * 128:(dc + 1) * 128],
                        wdq_hbm[dc * 128:(dc + 1) * 128, ic * 128:(ic + 1) * 128]),
                        qsem, 16, bw)
                wgq_loaded.append((sch.n("spq"), sch.n("actq")))

            NSC = S // 256
            ICH = [(ii * 512, min(512, I - ii * 512)) for ii in range((I + 511) // 512)]
            TQ = [("sp", "spq"), ("act", "actq")]
            silu_free = [barrier["dve"]]
            gps_free = [barrier["act"]]
            ups_free = [barrier["dve"]]
            hq_free = [dict(pe=0) for _ in range(2)]
            tp_copy = [barrier["act"], barrier["act"]]
            ffn_rd = [barrier["dve"], barrier["dve"]]
            ffn_free = [0, 0]
            hq_used = 0
            xqt_free = [0, 0]
            tq_cycle = 0
            nq2 = 0
            for sc in range(NSC):
                j = sc % 2
                for dc in range(NDC):
                    qeng, qsem = (("sp", "spq") if dc % 2 == 0 else ("act", "actq"))
                    for hh in range(2):
                        sch.emit(qeng, lambda e, dc=dc, sc=sc, hh=hh, j=j:
                                 e.dma_start_transpose(
                            xqt[j][dc][:, hh * 128:(hh + 1) * 128],
                            xq_hbm[sc * 256 + hh * 128:sc * 256 + (hh + 1) * 128,
                                   dc * 128:(dc + 1) * 128]), qsem, 16,
                            bw + [("pe", xqt_free[j])])
                xq_ld = (sch.n("spq"), sch.n("actq"))
                pend = []

                def flush_pend(qd_override=None):
                    nonlocal tq_cycle
                    if not pend:
                        return
                    jqf = pend[0][2]
                    for (ic, thf, jqp, q, qdp) in pend:
                        tj = tq_cycle % 2
                        tq_cycle += 1
                        qdw = qd_override if qd_override is not None else qdp
                        sch.emit("pe", lambda e, jqp=jqp, q=q, tj=tj: e.transpose(
                            tpps[tj][:], hq2[jqp][:, q * 128:(q + 1) * 128],
                            identb[:]),
                            "pe", 1, [("dve", qdw), ("act", tp_copy[tj])])
                        tn = sch.n("pe")
                        sch.emit("act", lambda e, ic=ic, thf=thf, tj=tj: e.copy(
                            hqT[ic][:, thf * 128:(thf + 1) * 128], tpps[tj][:]),
                            "act", 1, [("pe", tn)])
                        tp_copy[tj] = sch.n("act")
                    hq_free[jqf] = dict(pe=sch.n("pe"))
                    pend.clear()

                for th in range(2):
                    for ich, (ioff, W) in enumerate(ICH):
                        needw = wgq_loaded[(ioff + W - 1) // 128]
                        gw = [("spq", max(needw[0], xq_ld[0])),
                              ("actq", max(needw[1], xq_ld[1])),
                              ("act", gps_free[0])]
                        for dc in range(NDC):
                            sch.emit("pe", lambda e, dc=dc, ioff=ioff, W=W, th=th, j=j:
                                     e.matmul(
                                gps[:, 0:W], xqt[j][dc][:, th * 128:(th + 1) * 128],
                                WGQT[dc][:, ioff:ioff + W],
                                start=(dc == 0), stop=(dc == NDC - 1)), "pe", 1,
                                gw if dc == 0 else ())
                        gmm = sch.n("pe")
                        sch.emit("act", lambda e, W=W: e.activation(
                            sg[:, 0:W], gps[:, 0:W], AF.Silu),
                            "act", 1, [("pe", gmm), ("dve", silu_free[0])])
                        silu_done = sch.n("act")
                        gps_free[0] = silu_done
                        for dc in range(NDC):
                            sch.emit("pe", lambda e, dc=dc, ioff=ioff, W=W, th=th, j=j:
                                     e.matmul(
                                ups[:, 0:W], xqt[j][dc][:, th * 128:(th + 1) * 128],
                                WUQT[dc][:, ioff:ioff + W],
                                start=(dc == 0), stop=(dc == NDC - 1)), "pe", 1,
                                [("dve", ups_free[0])] if dc == 0 else ())
                        umm = sch.n("pe")
                        sch.emit("dve", lambda e, W=W: e.tensor_tensor(
                            hbuf[:, 0:W], sg[:, 0:W], ups[:, 0:W], ALU.mult),
                            "dve", 1, [("pe", umm), ("act", silu_done)])
                        silu_free[0] = sch.n("dve")
                        ups_free[0] = sch.n("dve")
                        hb = sch.n("dve")
                        jq = nq2 % 2
                        nq2 += 1
                        quant256(sch, hbuf[:, 0:W], hq2[jq][:, 0:W],
                                 [("dve", hb)],
                                 [("pe", hq_free[jq].get("pe", 0)),
                                  ("poolq", hq_free[jq].get("poolq", 0))],
                                 qtb[jq])
                        qd = sch.n("dve")
                        if dbg:
                            sch.emit("pool", lambda e, sc=sc, th=th, ioff=ioff,
                                     W=W, jq=jq: e.dma_start(
                                hq_hbm[sc * 256 + th * 128:sc * 256 + (th + 1) * 128,
                                       ioff:ioff + W], hq2[jq][:, 0:W]),
                                "poolq", 16, [("dve", qd)])
                            hq_free[jq]["poolq"] = sch.n("poolq")
                        flush_pend()
                        for q in range(W // 128):
                            pend.append((ioff // 128 + q, th, jq, q, qd))
                for _ in range(3):
                    sch.emit("dve", lambda e: e.tensor_copy(
                        settleB[:], hq2[0][:, 0:64]), "dve", 1)
                flush_pend(qd_override=sch.n("dve"))
                tp_done = sch.n("act")
                for sub in range(2):
                    dj = sub
                    dw = [("act", tp_done), ("dve", ffn_rd[dj])]
                    for ic in range(NIC):
                        for hh in range(2):
                            sch.emit("pe", lambda e, ic=ic, hh=hh, sub=sub, dj=dj: e.matmul(
                                dps[dj][:, hh * 512:(hh + 1) * 512],
                                hqT[ic][:, sub * 128:(sub + 1) * 128],
                                WDQT[ic][:, hh * 512:(hh + 1) * 512],
                                start=(ic == 0), stop=(ic == NIC - 1)), "pe", 1,
                                dw if (ic == 0 and hh == 0) else ())
                    dmm = sch.n("pe")
                    sch.emit("dve", lambda e, dj=dj: e.tensor_copy(ffnst[dj][:], dps[dj][:]),
                             "dve", 1, [("pe", dmm), ("poolq", ffn_free[dj])])
                    ffn_rd[dj] = sch.n("dve")
                    settle(sch, settleB[:], ffnst[dj][:, 0:64])
                    dvn = sch.n("dve")
                    sch.emit("pool", lambda e, sc=sc, sub=sub, dj=dj: e.dma_start(
                        ffn_hbm[sc * 256 + sub * 128:sc * 256 + (sub + 1) * 128, :],
                        ffnst[dj][:]), "poolq", 16, [("dve", dvn)])
                    ffn_free[dj] = sch.n("poolq")
                hq_used = sch.n("pe")
                xqt_free[j] = sch.n("pe")

            barrierB = {s: sch.n(s) for s in ALLSEMS}
            with nc.Block() as blockB:
                sch.run(blockB, sems)

        with ExitStack() as sc_:
            en = sc_.enter_context
            ow_bf = [en(nc.sbuf_tensor(f"owbf{d}", [128, D], BF16)) for d in range(NDC)]
            ow_st = [en(nc.sbuf_tensor(f"owst{j}", [128, D], F32)) for j in range(2)]
            aot_sl = [en(nc.sbuf_tensor(f"aotsl{j}", [128, D], BF16)) for j in range(2)]
            xc2 = [en(nc.sbuf_tensor(f"xc2_{j}", [128, D], F32)) for j in range(2)]
            ffnt = [en(nc.sbuf_tensor(f"ffnt{j}", [128, D], BF16)) for j in range(2)]
            obbc = en(nc.sbuf_tensor("obbc", [128, D], F32))
            obrow = en(nc.sbuf_tensor("obrow", [1, D], F32))
            outst = [en(nc.sbuf_tensor(f"outst{j}", [128, D], F32)) for j in range(2)]
            onec = en(nc.sbuf_tensor("onec", [1, 128], F32))
            settleC = en(nc.sbuf_tensor("settleC", [128, 64], F32))
            ops_ = [en(nc.psum_tensor(f"ops{j}", [128, D], F32)) for j in range(2)]

            bwc = [(s, barrierB[s]) for s in ALLSEMS]
            sch.emit("pool", lambda e: e.memset(onec[:], 1.0), "pool", 1, bwc)
            onecd = sch.n("pool")
            sch.emit("sp", lambda e: e.dma_start(obrow[:], ob_h.unsqueeze(0)), "spq", 16, bwc)
            obl = sch.n("spq")
            sch.emit("pe", lambda e: e.matmul(ops_[0][:, 0:512], onec[:], obrow[:, 0:512],
                                              start=True, stop=True), "pe", 1,
                     bwc + [("spq", obl), ("pool", onecd)])
            sch.emit("pe", lambda e: e.matmul(ops_[0][:, 512:1024], onec[:],
                                              obrow[:, 512:1024],
                                              start=True, stop=True), "pe", 1)
            pmm = sch.n("pe")
            sch.emit("dve", lambda e: e.tensor_copy(obbc[:], ops_[0][:]), "dve", 1,
                     bwc + [("pe", pmm)])
            obbcd = sch.n("dve")
            owc = []
            for dc in range(NDC):
                j = dc % 2
                wt = bwc + ([("dve", owc[dc - 2])] if dc >= 2 else [])
                sch.emit("sp", lambda e, dc=dc, j=j: e.dma_start(
                    ow_st[j][:], ow_h[dc * 128:(dc + 1) * 128, :]), "spq", 16, wt)
                ld = sch.n("spq")
                sch.emit("dve", lambda e, dc=dc, j=j: e.tensor_copy(ow_bf[dc][:],
                                                                    ow_st[j][:]),
                         "dve", 1, [("spq", ld)])
                owc.append(sch.n("dve"))
            ow_done = sch.n("dve")

            ld_free = [(0, 0), (0, 0)]
            out_free = [0, 0]
            out_rd = [obbcd, 0]
            for st in range(ST):
                j = st % 2
                for dc in range(NDC):
                    qeng, qsem = (("sp", "spq") if dc % 2 == 0 else ("act", "actq"))
                    sch.emit(qeng, lambda e, st=st, dc=dc, j=j: e.dma_start(
                        aot_sl[j][:, dc * 128:(dc + 1) * 128],
                        aot_hbm[dc * 128:(dc + 1) * 128, st * 128:(st + 1) * 128]),
                        qsem, 16, bwc + [("pe", ld_free[j][0])])
                sch.emit("sp", lambda e, st=st, j=j: e.dma_start(
                    xc2[j][:], x_h[st * 128:(st + 1) * 128, :]), "spq", 16,
                    bwc + [("dve", ld_free[j][1])])
                sch.emit("act", lambda e, st=st, j=j: e.dma_start(
                    ffnt[j][:], ffn_hbm[st * 128:(st + 1) * 128, :]), "actq", 16,
                    bwc + [("dve", ld_free[j][1])])
                lded = (sch.n("spq"), sch.n("actq"))
                mw = [("spq", lded[0]), ("actq", lded[1]), ("dve", ow_done),
                      ("dve", out_rd[j])]
                for dc in range(NDC):
                    for hh in range(2):
                        sch.emit("pe", lambda e, dc=dc, hh=hh, j=j: e.matmul(
                            ops_[j][:, hh * 512:(hh + 1) * 512],
                            aot_sl[j][:, dc * 128:(dc + 1) * 128],
                            ow_bf[dc][:, hh * 512:(hh + 1) * 512],
                            start=(dc == 0), stop=(dc == NDC - 1)), "pe", 1,
                            mw if (dc == 0 and hh == 0) else ())
                omm = sch.n("pe")
                sch.emit("dve", lambda e, j=j: e.tensor_tensor(
                    outst[j][:], ops_[j][:], xc2[j][:], ALU.add), "dve", 1,
                    [("pe", omm), ("actq", out_free[j])])
                sch.emit("dve", lambda e, j=j: e.tensor_tensor(
                    outst[j][:], outst[j][:], ffnt[j][:], ALU.add), "dve", 1)
                sch.emit("dve", lambda e, j=j: e.tensor_tensor(
                    outst[j][:], outst[j][:], obbc[:], ALU.add), "dve", 1)
                ld_free[j] = (omm, sch.n("dve"))
                out_rd[j] = sch.n("dve")
                settle(sch, settleC[:], outst[j][:, 0:64])
                wrd = sch.n("dve")
                sch.emit("act", lambda e, st=st, j=j: e.dma_start(
                    out_h[st * 128:(st + 1) * 128, :], outst[j][:]), "actq", 16,
                    [("dve", wrd)])
                out_free[j] = sch.n("actq")
            fin = sch.n("actq")
            sch.emit("pool", lambda e: e.engine_nop(), None, 0, [("actq", fin)])
            with nc.Block() as blockC:
                sch.run(blockC, sems)

    return nc

from concourse.bass_utils import run_bass_kernel_spmd

_NC_CACHE = {}


def kernel(L0_attn_x, L0_attn_q_w, L0_attn_q_b, L0_attn_k_w, L0_attn_k_b,
           L0_attn_v_w, L0_attn_v_b, L0_attn_o_w, L0_attn_o_b,
           L0_ffn_w_gate, L0_ffn_w_up, L0_ffn_w_down):
    x = np.ascontiguousarray(np.asarray(L0_attn_x, dtype=np.float32))
    B, S, Dm = x.shape
    assert (S, Dm) == (4096, 1024) and B == 8
    if "nc" not in _NC_CACHE:
        _NC_CACHE["nc"] = build_program(S=S)
    nc = _NC_CACHE["nc"]
    w = dict(
        qw=np.ascontiguousarray(L0_attn_q_w, dtype=np.float32),
        qb=np.ascontiguousarray(L0_attn_q_b, dtype=np.float32),
        kw=np.ascontiguousarray(L0_attn_k_w, dtype=np.float32),
        kb=np.ascontiguousarray(L0_attn_k_b, dtype=np.float32),
        vw=np.ascontiguousarray(L0_attn_v_w, dtype=np.float32),
        vb=np.ascontiguousarray(L0_attn_v_b, dtype=np.float32),
        ow=np.ascontiguousarray(L0_attn_o_w, dtype=np.float32),
        ob=np.ascontiguousarray(L0_attn_o_b, dtype=np.float32),
        wg=np.ascontiguousarray(L0_ffn_w_gate, dtype=np.float32),
        wu=np.ascontiguousarray(L0_ffn_w_up, dtype=np.float32),
        wd=np.ascontiguousarray(L0_ffn_w_down, dtype=np.float32),
    )
    import ml_dtypes
    ident = np.eye(128, dtype=ml_dtypes.bfloat16)
    in_maps = [dict(x=np.ascontiguousarray(x[c]), ident=ident, **w) for c in range(B)]
    res = run_bass_kernel_spmd(nc, in_maps, core_ids=list(range(B)))
    return np.stack([np.asarray(res.results[c]["out"], dtype=np.float32)
                     for c in range(B)], axis=0)


# version 12 (final)
# speedup vs baseline: 1.1880x; 1.1880x over previous
import numpy as np
from contextlib import ExitStack
import concourse.bass as bass
import concourse.mybir as mybir

F32 = mybir.dt.float32
BF16 = mybir.dt.bfloat16
FP8 = mybir.dt.float8e4
U16 = mybir.dt.uint16
U32 = mybir.dt.uint32
AF = mybir.ActivationFunctionType
ALU = mybir.AluOpType
AX = mybir.AxisListType

A4 = 1.0 + 0.95 + 0.95**2 + 0.95**3
TH = [float(np.float32(1.0 / A4)), float(np.float32(1.95 / A4)),
      float(np.float32(2.95 / A4)), float(np.float32(3.8525 / A4))]
MAGIC = float(np.float32(1.5 * 2**23))
SIXTH = float(np.float32(1.0 / 6.0))

D = 1024
I = 2816
NDC = 8
NIC = 22
NP = 8

ENGINES = ("sp", "pe", "dve", "act", "pool")
SETTLE_OPS = 4
ALLSEMS = ("spq", "actq", "poolq", "pe", "dve", "act", "pool")


class Sched:
    def __init__(self):
        self.ops = {e: [] for e in ENGINES}
        self.cnt = {}
        self.lastwait = {}

    def n(self, sem):
        return self.cnt.get(sem, 0)

    SELF_SYNC = {"dve": "dve", "act": "act", "pool": "pool"}

    def emit(self, eng, fn, sem=None, inc=0, waits=()):
        w = []
        ss = self.SELF_SYNC.get(eng)
        if ss is not None and self.n(ss) > 0:
            waits = list(waits) + [(ss, self.n(ss))]
        for (s, v) in waits:
            if v is None or v <= 0:
                continue
            key = (eng, s)
            if self.lastwait.get(key, -1) >= v:
                continue
            self.lastwait[key] = v
            w.append((s, v))
        self.ops[eng].append((fn, sem, inc, tuple(w)))
        if sem is not None:
            self.cnt[sem] = self.cnt.get(sem, 0) + inc

    def run(self, block, sems):
        secmap = {"sp": block.sync, "pe": block.tensor, "dve": block.vector,
                  "act": block.scalar, "pool": block.gpsimd}
        for ename in ENGINES:
            oplist = self.ops[ename]
            if not oplist:
                continue

            def body(eng, oplist=oplist):
                for (fn, sem, inc, waits) in oplist:
                    for (s, v) in waits:
                        eng.wait_ge(sems[s], v)
                    ins = fn(eng)
                    if sem is not None:
                        ins.then_inc(sems[sem], inc)
            secmap[ename](body)
        self.ops = {e: [] for e in ENGINES}


def settle(sch, dst, src):
    for _ in range(SETTLE_OPS):
        sch.emit("dve", lambda e: e.tensor_copy(dst, src), "dve", 1)


def quant256(sch, src_ap, dst_ap, src_waits, dst_waits, ts):
    W = src_ap.shape[1]
    nb = W // 16
    w3 = src_ap.rearrange("p (b c) -> p b c", c=16)
    sw = list(src_waits)
    sch.emit("dve", lambda e: e.tensor_reduce(
        ts["blk"][:, 0:nb], w3, AX.X, ALU.max, apply_absolute_value=True),
        "dve", 1, sw + [("dve", ts["busy_dve"])])
    sch.emit("dve", lambda e: e.tensor_scalar(
        ts["scl"][:, 0:nb], ts["blk"][:, 0:nb], SIXTH, 1e-12, ALU.mult, ALU.max),
        "dve", 1)
    sch.emit("dve", lambda e: e.reciprocal(ts["rcp"][:, 0:nb], ts["scl"][:, 0:nb]),
             "dve", 1)
    sch.emit("dve", lambda e: e.tensor_scalar(
        ts["r2"][:, 0:nb], ts["rcp"][:, 0:nb], 2.0, None, ALU.mult), "dve", 1)
    r2b = ts["r2"][:, 0:nb].unsqueeze(2).broadcast_to((128, nb, 16))
    sch.emit("dve", lambda e: e.tensor_tensor(
        ts["u2"][:, 0:W].rearrange("p (b c) -> p b c", c=16), w3, r2b, ALU.mult),
        "dve", 1, sw)
    sch.emit("dve", lambda e: e.tensor_scalar(
        ts["a2"][:, 0:W].bitcast(U32), ts["u2"][:, 0:W].bitcast(U32),
        0x7FFFFFFF, None, ALU.bitwise_and), "dve", 1)
    sch.emit("dve", lambda e: e.tensor_scalar(
        ts["c1"][:, 0:W], ts["a2"][:, 0:W], 5.0, None, ALU.is_gt), "dve", 1)
    sch.emit("dve", lambda e: e.tensor_scalar(
        ts["c2"][:, 0:W], ts["a2"][:, 0:W], 7.0, None, ALU.is_gt), "dve", 1)
    sch.emit("dve", lambda e: e.tensor_scalar(
        ts["c3"][:, 0:W], ts["a2"][:, 0:W], 10.0, 2.0, ALU.is_gt, ALU.mult),
        "dve", 1)
    sch.emit("dve", lambda e: e.tensor_scalar(
        ts["a2"][:, 0:W], ts["u2"][:, 0:W], MAGIC, MAGIC, ALU.add, ALU.subtract),
        "dve", 1)
    sch.emit("dve", lambda e: e.tensor_scalar(
        ts["u2"][:, 0:W], ts["a2"][:, 0:W], 4.0, None, ALU.min), "dve", 1)
    sch.emit("dve", lambda e: e.tensor_scalar(
        ts["q0"][:, 0:W], ts["u2"][:, 0:W], -4.0, None, ALU.max), "dve", 1)
    sch.emit("dve", lambda e: e.tensor_tensor(
        ts["sA"][:, 0:W], ts["c1"][:, 0:W], ts["c2"][:, 0:W], ALU.add),
        "dve", 1)
    sch.emit("dve", lambda e: e.tensor_tensor(
        ts["c2"][:, 0:W], ts["sA"][:, 0:W], ts["c3"][:, 0:W], ALU.add), "dve", 1)
    sch.emit("dve", lambda e: e.tensor_scalar(
        ts["sA"][:, 0:W].bitcast(U16), ts["q0"][:, 0:W].bitcast(U16), 0x8000, None,
        ALU.bitwise_and), "dve", 1)
    sch.emit("dve", lambda e: e.tensor_tensor(
        ts["c1"][:, 0:W].bitcast(U16), ts["c2"][:, 0:W].bitcast(U16),
        ts["sA"][:, 0:W].bitcast(U16), ALU.bitwise_or), "dve", 1)
    sch.emit("dve", lambda e: e.scalar_tensor_tensor(
        ts["sA"][:, 0:W], ts["q0"][:, 0:W], 0.5, ts["c1"][:, 0:W],
        ALU.mult, ALU.add), "dve", 1)
    sclb = ts["scl"][:, 0:nb].unsqueeze(2).broadcast_to((128, nb, 16))
    sch.emit("dve", lambda e: e.tensor_tensor(
        dst_ap.rearrange("p (b c) -> p b c", c=16),
        ts["sA"][:, 0:W].rearrange("p (b c) -> p b c", c=16), sclb, ALU.mult),
        "dve", 1, list(dst_waits))
    ts["busy_dve"] = sch.n("dve")
    return None


def build_program(S=4096, dbg=False):
    ST = S // 128
    NQ = S // 512
    NK = S // 128
    nc = bass.Bass("TRN2", target_bir_lowering=False)

    x_h = nc.dram_tensor("x", [S, D], F32, kind="ExternalInput").ap()
    qw_h = nc.dram_tensor("qw", [D, D], F32, kind="ExternalInput").ap()
    qb_h = nc.dram_tensor("qb", [D], F32, kind="ExternalInput").ap()
    kw_h = nc.dram_tensor("kw", [D, D], F32, kind="ExternalInput").ap()
    kb_h = nc.dram_tensor("kb", [D], F32, kind="ExternalInput").ap()
    vw_h = nc.dram_tensor("vw", [D, D], F32, kind="ExternalInput").ap()
    vb_h = nc.dram_tensor("vb", [D], F32, kind="ExternalInput").ap()
    ow_h = nc.dram_tensor("ow", [D, D], F32, kind="ExternalInput").ap()
    ob_h = nc.dram_tensor("ob", [D], F32, kind="ExternalInput").ap()
    wg_h = nc.dram_tensor("wg", [I, D], F32, kind="ExternalInput").ap()
    wu_h = nc.dram_tensor("wu", [I, D], F32, kind="ExternalInput").ap()
    wd_h = nc.dram_tensor("wd", [D, I], F32, kind="ExternalInput").ap()
    out_h = nc.dram_tensor("out", [S, D], F32, kind="ExternalOutput").ap()
    ident_h = nc.dram_tensor("ident", [128, 128], BF16, kind="ExternalInput").ap()

    kw = dict(kind="ExternalOutput") if dbg else {}
    xq_hbm = nc.dram_tensor("xq_hbm", [S, D], BF16, **kw).ap()
    wgq_hbm = nc.dram_tensor("wgq_hbm", [I, D], BF16, **kw).ap()
    wuq_hbm = nc.dram_tensor("wuq_hbm", [I, D], BF16, **kw).ap()
    wdq_hbm = nc.dram_tensor("wdq_hbm", [D, I], BF16, **kw).ap()
    aot_hbm = nc.dram_tensor("aot_hbm", [D, S], BF16, **kw).ap()
    ffn_hbm = nc.dram_tensor("ffn_hbm", [S, D], BF16, **kw).ap()
    hq_hbm = nc.dram_tensor("hq_hbm", [S, I], BF16, **kw).ap() if dbg else None

    sch = Sched()

    def mkquanttemps(en, pfx):
        return [dict(
            blk=en(nc.sbuf_tensor(f"{pfx}blk{j}", [128, 32], F32)),
            scl=en(nc.sbuf_tensor(f"{pfx}scl{j}", [128, 32], F32)),
            rcp=en(nc.sbuf_tensor(f"{pfx}rcp{j}", [128, 32], F32)),
            r2=en(nc.sbuf_tensor(f"{pfx}r2{j}", [128, 32], F32)),
            u2=en(nc.sbuf_tensor(f"{pfx}u2{j}", [128, 512], F32)),
            a2=en(nc.sbuf_tensor(f"{pfx}a2{j}", [128, 512], F32)),
            q0=en(nc.sbuf_tensor(f"{pfx}q0{j}", [128, 512], BF16)),
            c1=en(nc.sbuf_tensor(f"{pfx}c1{j}", [128, 512], BF16)),
            c2=en(nc.sbuf_tensor(f"{pfx}c2{j}", [128, 512], BF16)),
            c3=en(nc.sbuf_tensor(f"{pfx}c3{j}", [128, 512], BF16)),
            sA=en(nc.sbuf_tensor(f"{pfx}sA{j}", [128, 512], BF16)),
            busy_dve=0,
        ) for j in range(2)]


    with ExitStack() as top:
        sems = {}
        for s in ALLSEMS:
            sems[s] = top.enter_context(nc.semaphore("sem_" + s))

        ones_l = top.enter_context(nc.sbuf_tensor("ones_l", [128, 512], F32))
        fours = top.enter_context(nc.sbuf_tensor("fours", [128, 1], BF16))
        biasrows = top.enter_context(nc.sbuf_tensor("biasrows", [65, D], F32))
        identb = top.enter_context(nc.sbuf_tensor("identb", [128, 128], BF16))

        sch.emit("sp", lambda e: e.dma_start(identb[:], ident_h), "spq", 16)
        sch.emit("pool", lambda e: e.memset(ones_l[:], 1.0), "pool", 1)
        sch.emit("pool", lambda e: e.memset(fours[:], 4.0), "pool", 1)
        const_pool = sch.n("pool")
        sch.emit("sp", lambda e: e.dma_start(biasrows[0:1, :], qb_h.unsqueeze(0)), "spq", 16)
        sch.emit("sp", lambda e: e.dma_start(biasrows[32:33, :], kb_h.unsqueeze(0)), "spq", 16)
        sch.emit("sp", lambda e: e.dma_start(biasrows[64:65, :], vb_h.unsqueeze(0)), "spq", 16)
        bias_ld = sch.n("spq")

        with ExitStack() as sa:
            en = sa.enter_context
            xnT = [en(nc.sbuf_tensor(f"xnT{d}", [128, S], BF16)) for d in range(NDC)]
            v8 = [en(nc.sbuf_tensor(f"v8_{t}", [128, D], FP8)) for t in range(NK)]
            q8 = [en(nc.sbuf_tensor(f"q8_{j}", [128, S], FP8)) for j in range(2)]
            k8 = [en(nc.sbuf_tensor(f"k8_{j}", [128, S], FP8)) for j in range(2)]
            vbf = [en(nc.sbuf_tensor(f"vbf{t}", [128, 128], BF16)) for t in range(NK)]
            attnT = [en(nc.sbuf_tensor(f"attnT{j}", [128, 1024], BF16)) for j in range(3)]
            x2 = [en(nc.sbuf_tensor(f"x2_{j}", [128, D], F32)) for j in range(2)]
            xn = en(nc.sbuf_tensor("xn", [128, D], F32))
            xnbf = [en(nc.sbuf_tensor(f"xnbf{j}", [128, D], BF16)) for j in range(2)]
            vw_bf = [en(nc.sbuf_tensor(f"vwbf{d}", [128, D], BF16)) for d in range(NDC)]
            qkw_st = [en(nc.sbuf_tensor(f"qkwst{j}", [128, 512], F32)) for j in range(2)]
            qw_bf = en(nc.sbuf_tensor("qw_bf", [128, D], BF16))
            kw_bf = en(nc.sbuf_tensor("kw_bf", [128, D], BF16))
            wstage = en(nc.sbuf_tensor("wstage", [128, 512], F32))
            wqstage = en(nc.sbuf_tensor("wqstage", [128, 512], BF16))
            xqstage = [en(nc.sbuf_tensor(f"xqst{j}", [128, D], BF16)) for j in range(2)]
            serec = en(nc.sbuf_tensor("serec", [65, 512], F32))
            rb_sb = en(nc.sbuf_tensor("rb_sb", [128, 512], F32))
            aotst = [en(nc.sbuf_tensor(f"aotst{j}", [128, 512], BF16)) for j in range(2)]
            lt = [en(nc.sbuf_tensor(f"lt{j}", [128, 512], BF16)) for j in range(2)]
            bn6 = en(nc.sbuf_tensor("bn6", [128, 12], F32))
            bn2 = en(nc.sbuf_tensor("bn2", [128, 2], F32))
            rstd = en(nc.sbuf_tensor("rstd", [128, 1], F32))
            negmr = en(nc.sbuf_tensor("negmr", [128, 1], F32))
            settleA = en(nc.sbuf_tensor("settleA", [128, 64], F32))
            qt = mkquanttemps(en, "qa")
            sps = [en(nc.psum_tensor(f"sps{j}", [128, 1024], F32)) for j in range(2)]
            avps = en(nc.psum_tensor("avps", [128, 512], F32))
            seps = en(nc.psum_tensor("seps", [65, 512], F32))
            rbps = en(nc.psum_tensor("rbps", [128, 512], F32))
            pps = en(nc.psum_tensor("pps", [128, 512], F32))

            wplan = []
            for (srch, dsth, rows, cols) in ((wg_h, wgq_hbm, NIC, D), (wu_h, wuq_hbm, NIC, D)):
                for r in range(rows):
                    for c in range(cols // 512):
                        wplan.append((srch[r * 128:(r + 1) * 128, c * 512:(c + 1) * 512],
                                      dsth[r * 128:(r + 1) * 128, c * 512:(c + 1) * 512]))
            for r in range(NDC):
                for c in range(5):
                    wplan.append((wd_h[r * 128:(r + 1) * 128, c * 512:(c + 1) * 512],
                                  wdq_hbm[r * 128:(r + 1) * 128, c * 512:(c + 1) * 512]))
                wplan.append((wd_h[r * 128:(r + 1) * 128, I - 256:I],
                              wdq_hbm[r * 128:(r + 1) * 128, I - 256:I]))

            wstate = dict(i=0, ld=0, stage_free_dve=0, store_done=0, nq=0)

            def pump_wprep(nunits):
                for _ in range(nunits):
                    if wstate["i"] >= len(wplan):
                        return
                    i = wstate["i"]
                    src, dst = wplan[i]
                    width = src.shape[1]
                    sch.emit("sp", lambda e, src=src, width=width: e.dma_start(
                        wstage[:, 0:width], src), "spq", 16,
                        [("dve", wstate["stage_free_dve"])])
                    wstate["ld"] = sch.n("spq")
                    ts = qt[wstate["nq"] % 2]
                    wstate["nq"] += 1
                    quant256(sch, wstage[:, 0:width], wqstage[:, 0:width],
                             [("spq", wstate["ld"])],
                             [("actq", wstate["store_done"])], ts)
                    wstate["stage_free_dve"] = sch.n("dve")
                    settle(sch, settleA[:], wqstage[:, 0:64])
                    dvn = sch.n("dve")
                    sch.emit("act", lambda e, dst=dst, width=width: e.dma_start(
                        dst, wqstage[:, 0:width]), "actq", 16, [("dve", dvn)])
                    wstate["store_done"] = sch.n("actq")
                    wstate["i"] += 1

            vw_marks = []
            for u in range(16):
                dc, half = divmod(u, 2)
                j = u % 2
                wt = [("dve", vw_marks[u - 2])] if u >= 2 else []
                sch.emit("sp", lambda e, dc=dc, half=half, j=j: e.dma_start(
                    qkw_st[j][:], vw_h[dc * 128:(dc + 1) * 128, half * 512:(half + 1) * 512]),
                    "spq", 16, wt)
                ld = sch.n("spq")
                sch.emit("dve", lambda e, dc=dc, half=half, j=j: e.tensor_copy(
                    vw_bf[dc][:, half * 512:(half + 1) * 512], qkw_st[j][:]),
                    "dve", 1, [("spq", ld)])
                vw_marks.append(sch.n("dve"))
            vw_cast_done = sch.n("dve")

            x_rd = [0, 0]
            xnbf_free = [(0, 0), (0, 0)]
            xqst_free = [0, 0]
            xnT_done = []
            vlif_done = 0

            for st in range(ST):
                j = st % 2
                sch.emit("sp", lambda e, st=st, j=j: e.dma_start(
                    x2[j][:], x_h[st * 128:(st + 1) * 128, :]), "spq", 16,
                    [("dve", x_rd[j])])
                xld = sch.n("spq")
                sch.emit("dve", lambda e, j=j: e.bn_stats(bn6[:, 0:6], x2[j][:, 0:512]),
                         "dve", 1, [("spq", xld)])
                sch.emit("dve", lambda e, j=j: e.bn_stats(bn6[:, 6:12], x2[j][:, 512:1024]),
                         "dve", 1)
                sch.emit("dve", lambda e: e.bn_aggr(bn2[:], bn6[:]), "dve", 1)
                bnd = sch.n("dve")
                sch.emit("act", lambda e: e.activation(rstd[:], bn2[:, 1:2], AF.Ln, bias=np.float32(1e-5)),
                         "act", 1, [("dve", bnd)])
                sch.emit("act", lambda e: e.activation(rstd[:], rstd[:], AF.Exp, scale=np.float32(-0.5)),
                         "act", 1)
                rstd_done = sch.n("act")
                sch.emit("dve", lambda e: e.scalar_tensor_tensor(
                    negmr[:], bn2[:, 0:1], -1.0, rstd[:], ALU.mult, ALU.mult),
                    "dve", 1, [("act", rstd_done)])
                sch.emit("dve", lambda e, j=j: e.tensor_scalar(
                    xn[:], x2[j][:], rstd[:], negmr[:], ALU.mult, ALU.add), "dve", 1)
                sch.emit("dve", lambda e, j=j: e.tensor_copy(xnbf[j][:], xn[:]), "dve", 1,
                         [("spq", xnbf_free[j][0]), ("actq", xnbf_free[j][1])])
                xn_done = sch.n("dve")
                x_rd[j] = xn_done
                for dc in range(NDC):
                    qeng, qsem = (("sp", "spq") if dc % 2 == 0 else ("act", "actq"))
                    sch.emit(qeng, lambda e, dc=dc, st=st, j=j: e.dma_start_transpose(
                        xnT[dc][:, st * 128:(st + 1) * 128],
                        xnbf[j][:, dc * 128:(dc + 1) * 128]), qsem, 16,
                        [("dve", xn_done)])
                xnbf_free[j] = (sch.n("spq"), sch.n("actq"))
                xnT_done.append((sch.n("spq"), sch.n("actq")))
                for c in range(2):
                    ts = qt[c % 2]
                    quant256(sch,
                        xn[:, c * 512:(c + 1) * 512], xqstage[j][:, c * 512:(c + 1) * 512],
                        [("dve", xn_done)], [("actq", xqst_free[j])], ts)
                settle(sch, settleA[:], xqstage[j][:, 0:64])
                dvn = sch.n("dve")
                sch.emit("act", lambda e, st=st, j=j: e.dma_start(
                    xq_hbm[st * 128:(st + 1) * 128, :], xqstage[j][:]), "actq", 16,
                    [("dve", dvn)])
                xqst_free[j] = sch.n("actq")
                tw = xnT_done[st]
                vwaits = [("spq", max(tw[0], bias_ld)), ("actq", tw[1]),
                          ("dve", max(vw_cast_done, vlif_done))]
                for half in range(2):
                    sch.emit("pe", lambda e, half=half: e.matmul(
                        sps[0][:, half * 512:(half + 1) * 512], ones_l[64:65, 0:128],
                        biasrows[64:65, half * 512:(half + 1) * 512],
                        start=True, stop=False, tile_position=(64, 0)), "pe", 1,
                        vwaits if half == 0 else ())
                for dc in range(NDC):
                    for half in range(2):
                        sch.emit("pe", lambda e, dc=dc, half=half, st=st: e.matmul(
                            sps[0][:, half * 512:(half + 1) * 512],
                            xnT[dc][:, st * 128:(st + 1) * 128],
                            vw_bf[dc][:, half * 512:(half + 1) * 512],
                            start=False, stop=(dc == NDC - 1)), "pe", 1)
                vproj = sch.n("pe")
                for half in range(2):
                    pv = sps[0][:, half * 512:(half + 1) * 512]
                    sch.emit("dve", lambda e, pv=pv: e.tensor_scalar(
                        lt[0][:], pv, TH[0], None, ALU.is_gt), "dve", 1, [("pe", vproj)])
                    sch.emit("dve", lambda e, pv=pv: e.scalar_tensor_tensor(
                        lt[1][:], pv, TH[1], lt[0][:], ALU.is_gt, ALU.add), "dve", 1)
                    sch.emit("dve", lambda e, pv=pv: e.scalar_tensor_tensor(
                        lt[0][:], pv, TH[2], lt[1][:], ALU.is_gt, ALU.add), "dve", 1)
                    sch.emit("dve", lambda e, pv=pv, st=st, half=half: e.scalar_tensor_tensor(
                        v8[st][:, half * 512:(half + 1) * 512], pv, TH[3], lt[0][:],
                        ALU.is_gt, ALU.add), "dve", 1)
                vlif_done = sch.n("dve")

            all_xnT = (sch.n("spq"), sch.n("actq"))

            projq_done = [0, 0]
            projk_done = [0, 0]
            q_freed = [0, 0]
            projlif_guard = [vlif_done]
            qkst_guard = [0]
            last_proj_pe = [0]

            def emit_proj_load(p):
                for (w_h, wbf, base) in ((qw_h, qw_bf, 0), (kw_h, kw_bf, 1)):
                    src = w_h[:, p * 128:(p + 1) * 128].rearrange("(c p) f -> p c f", p=128)
                    for half in range(2):
                        j = half
                        lo = half * 4
                        sch.emit("sp", lambda e, src=src, lo=lo, j=j: e.dma_start(
                            qkw_st[j][:].rearrange("p (c f) -> p c f", f=128),
                            src[:, lo:lo + 4, :]), "spq", 16, [("dve", qkst_guard[0])])
                        ld = sch.n("spq")
                        sch.emit("dve", lambda e, wbf=wbf, lo=lo, j=j: e.tensor_copy(
                            wbf[:, lo * 128:(lo + 4) * 128], qkw_st[j][:]), "dve", 1,
                            [("spq", ld), ("pe", last_proj_pe[0])])
                qkst_guard[0] = sch.n("dve")

            def emit_proj_unit(p, qc, which):
                par = p % 2
                isq = which == "q"
                wbf = qw_bf if isq else kw_bf
                dst = (q8 if isq else k8)[par]
                base = 0 if isq else 32
                tp = (0, 0) if base == 0 else (32, 0)
                waits = [("dve", max(projlif_guard[0], qkst_guard[0])),
                         ("spq", max(all_xnT[0], bias_ld)), ("actq", all_xnT[1])]
                sch.emit("pe", lambda e, base=base, p=p, tp=tp: e.matmul(
                    pps[:], biasrows[base:base + 1, p * 128:(p + 1) * 128],
                    ones_l[base:base + 1, 0:512],
                    start=True, stop=False, tile_position=tp), "pe", 1, waits)
                for dc in range(NDC):
                    sch.emit("pe", lambda e, dc=dc, qc=qc, wbf=wbf: e.matmul(
                        pps[:], wbf[:, dc * 128:(dc + 1) * 128],
                        xnT[dc][:, qc * 512:(qc + 1) * 512],
                        start=False, stop=(dc == NDC - 1)), "pe", 1)
                mm = sch.n("pe")
                last_proj_pe[0] = mm
                sch.emit("dve", lambda e: e.tensor_scalar(
                    lt[0][:], pps[:], TH[0], None, ALU.is_gt), "dve", 1, [("pe", mm)])
                sch.emit("dve", lambda e: e.scalar_tensor_tensor(
                    lt[1][:], pps[:], TH[1], lt[0][:], ALU.is_gt, ALU.add), "dve", 1)
                sch.emit("dve", lambda e: e.scalar_tensor_tensor(
                    lt[0][:], pps[:], TH[2], lt[1][:], ALU.is_gt, ALU.add), "dve", 1)
                sch.emit("dve", lambda e, dst=dst, qc=qc: e.scalar_tensor_tensor(
                    dst[:, qc * 512:(qc + 1) * 512], pps[:], TH[3], lt[0][:],
                    ALU.is_gt, ALU.add), "dve", 1)
                projlif_guard[0] = sch.n("dve")
                if isq:
                    projq_done[par] = sch.n("dve")
                else:
                    projk_done[par] = sch.n("dve")

            def emit_vbf(p):
                for t in range(NK):
                    sch.emit("dve", lambda e, t=t, p=p: e.tensor_copy(
                        vbf[t][:], v8[t][:, p * 128:(p + 1) * 128]), "dve", 1,
                        [("pe", vbf_guard[0])])
                return sch.n("dve")

            exp_done = [0, 0]
            attnT_used = [0, 0, 0]
            epi_guard = [0]
            vbf_guard = [0]
            aot_free = [0, 0]

            emit_proj_load(0)
            for qc in range(NQ):
                emit_proj_unit(0, qc, "q")
                emit_proj_unit(0, qc, "k")
            vbf_ready = emit_vbf(0)

            for p in range(NP):
                par = p % 2
                nxt = ([(p + 1, qc, w) for qc in range(NQ) for w in ("q", "k")]
                       if p < NP - 1 else [])
                nxt_loaded = False
                for qc in range(NQ):
                    qwait = ("dve", max(projq_done[par], projk_done[par], vbf_ready))
                    for kt in range(NK):
                        jj = kt % 2
                        jjj = kt % 3
                        sch.emit("pe", lambda e, qc=qc, kt=kt, jj=jj, par=par: e.matmul(
                            sps[jj][:, 0:512], k8[par][0:64, kt * 128:(kt + 1) * 128],
                            q8[par][0:64, qc * 512:(qc + 1) * 512],
                            start=True, stop=True, tile_position=(0, 0)), "pe", 1,
                            [qwait, ("act", exp_done[jj])])
                        sch.emit("pe", lambda e, qc=qc, kt=kt, jj=jj, par=par: e.matmul(
                            sps[jj][:, 512:1024], k8[par][64:128, kt * 128:(kt + 1) * 128],
                            q8[par][64:128, qc * 512:(qc + 1) * 512],
                            start=True, stop=True, tile_position=(64, 0)), "pe", 1)
                        scdone = sch.n("pe")
                        sch.emit("act", lambda e, jj=jj, jjj=jjj: e.activation(
                            attnT[jjj][:], sps[jj][:], AF.Exp, scale=np.float32(1.0 / 128.0)),
                            "act", 1, [("pe", scdone), ("pe", attnT_used[jjj])])
                        exp_done[jj] = sch.n("act")
                        avw = [("act", exp_done[jj])]
                        if kt == 0:
                            avw.append(("dve", epi_guard[0]))
                        sch.emit("pe", lambda e, kt=kt, jjj=jjj: e.matmul(
                            avps[0:64, :], vbf[kt][:, 0:64], attnT[jjj][:, 0:512],
                            start=(kt == 0), stop=(kt == NK - 1),
                            tile_position=(0, 0)), "pe", 1, avw)
                        sch.emit("pe", lambda e, kt=kt, jjj=jjj: e.matmul(
                            avps[64:128, :], vbf[kt][:, 64:128], attnT[jjj][:, 512:1024],
                            start=(kt == 0), stop=(kt == NK - 1),
                            tile_position=(0, 64)), "pe", 1)
                        sew = ([("dve", epi_guard[0]), ("pool", const_pool)]
                               if kt == 0 else ())
                        sch.emit("pe", lambda e, kt=kt, jjj=jjj: e.matmul(
                            seps[0:1, :], fours[:], attnT[jjj][:, 0:512],
                            start=(kt == 0), stop=(kt == NK - 1),
                            tile_position=(0, 0)), "pe", 1, sew)
                        sch.emit("pe", lambda e, kt=kt, jjj=jjj: e.matmul(
                            seps[32:33, :], fours[:], attnT[jjj][:, 512:1024],
                            start=(kt == 0), stop=(kt == NK - 1),
                            tile_position=(0, 32)), "pe", 1)
                        attnT_used[jjj] = sch.n("pe")
                    ktend = sch.n("pe")
                    sch.emit("dve", lambda e: e.reciprocal(serec[:], seps[:]), "dve", 1,
                             [("pe", ktend)])
                    rp = sch.n("dve")
                    sch.emit("pe", lambda e: e.matmul(
                        rbps[0:64, :], ones_l[0:1, 0:64], serec[0:1, :],
                        start=True, stop=True), "pe", 1, [("dve", rp)])
                    sch.emit("pe", lambda e: e.matmul(
                        rbps[64:128, :], ones_l[32:33, 0:64], serec[32:33, :],
                        start=True, stop=True, tile_position=(32, 64)), "pe", 1)
                    bc = sch.n("pe")
                    sch.emit("dve", lambda e: e.tensor_copy(rb_sb[:], rbps[:]), "dve", 1,
                             [("pe", bc)])
                    aj = qc % 2
                    sch.emit("dve", lambda e, aj=aj: e.tensor_tensor(
                        aotst[aj][:], avps[:], rb_sb[:], ALU.mult), "dve", 1,
                        [("actq", aot_free[aj])])
                    epi_guard[0] = sch.n("dve")
                    settle(sch, settleA[:], aotst[aj][:, 0:64])
                    aotd = sch.n("dve")
                    sch.emit("act", lambda e, p=p, qc=qc, aj=aj: e.dma_start(
                        aot_hbm[p * 128:(p + 1) * 128, qc * 512:(qc + 1) * 512],
                        aotst[aj][:]), "actq", 16, [("dve", aotd)])
                    aot_free[aj] = sch.n("actq")
                    if nxt and not nxt_loaded:
                        emit_proj_load(p + 1)
                        nxt_loaded = True
                    for _ in range(2):
                        if nxt:
                            pp, qq, ww = nxt.pop(0)
                            emit_proj_unit(pp, qq, ww)
                    pump_wprep(5)
                q_freed[par] = sch.n("pe")
                if p < NP - 1:
                    while nxt:
                        pp, qq, ww = nxt.pop(0)
                        emit_proj_unit(pp, qq, ww)
                    vbf_guard[0] = sch.n("pe")
                    vbf_ready = emit_vbf(p + 1)
            pump_wprep(10 ** 9)

            barrier = {s: sch.n(s) for s in ALLSEMS}
            with nc.Block() as blockA:
                sch.run(blockA, sems)

        with ExitStack() as sb:
            en = sb.enter_context
            WGQT = [en(nc.sbuf_tensor(f"WGQT{d}", [128, I], BF16)) for d in range(NDC)]
            WUQT = [en(nc.sbuf_tensor(f"WUQT{d}", [128, I], BF16)) for d in range(NDC)]
            WDQT = [en(nc.sbuf_tensor(f"WDQT{i}", [128, D], BF16)) for i in range(NIC)]
            xqt = [[en(nc.sbuf_tensor(f"xqt{j}_{d}", [128, 256], BF16))
                    for d in range(NDC)] for j in range(2)]
            sg = en(nc.sbuf_tensor("sg", [128, 512], F32))
            hbuf = en(nc.sbuf_tensor("hbuf", [128, 512], F32))
            hq2 = [en(nc.sbuf_tensor(f"hq2_{j}", [128, 512], BF16)) for j in range(2)]
            settleB = en(nc.sbuf_tensor("settleB", [128, 64], F32))
            hqT = [en(nc.sbuf_tensor(f"hqT{i}", [128, 256], BF16)) for i in range(NIC)]
            ffnst = [en(nc.sbuf_tensor(f"ffnst{j}", [128, D], BF16)) for j in range(2)]
            qtb = mkquanttemps(en, "qb")
            for t in qtb:
                t["busy_dve"] = barrier["dve"]
            gps = en(nc.psum_tensor("gps", [128, 512], F32))
            ups = en(nc.psum_tensor("ups", [128, 512], F32))
            dps = [en(nc.psum_tensor(f"dps{j}", [128, D], F32)) for j in range(2)]
            tpps = [en(nc.psum_tensor(f"tpps{j}", [128, 128], BF16)) for j in range(2)]

            bw = [(s, barrier[s]) for s in ALLSEMS]

            wgq_loaded = []
            for ic in range(NIC):
                for dc in range(NDC):
                    sch.emit("sp", lambda e, ic=ic, dc=dc: e.dma_start_transpose(
                        WGQT[dc][:, ic * 128:(ic + 1) * 128],
                        wgq_hbm[ic * 128:(ic + 1) * 128, dc * 128:(dc + 1) * 128]),
                        "spq", 16, bw)
                    sch.emit("act", lambda e, ic=ic, dc=dc: e.dma_start_transpose(
                        WUQT[dc][:, ic * 128:(ic + 1) * 128],
                        wuq_hbm[ic * 128:(ic + 1) * 128, dc * 128:(dc + 1) * 128]),
                        "actq", 16, bw)
                for dc in range(NDC):
                    qeng, qsem = (("sp", "spq") if dc % 2 == 0 else ("act", "actq"))
                    sch.emit(qeng, lambda e, ic=ic, dc=dc: e.dma_start_transpose(
                        WDQT[ic][:, dc * 128:(dc + 1) * 128],
                        wdq_hbm[dc * 128:(dc + 1) * 128, ic * 128:(ic + 1) * 128]),
                        qsem, 16, bw)
                wgq_loaded.append((sch.n("spq"), sch.n("actq")))

            NSC = S // 256
            ICH = [(ii * 512, min(512, I - ii * 512)) for ii in range((I + 511) // 512)]
            TQ = [("sp", "spq"), ("act", "actq")]
            silu_free = [barrier["dve"]]
            gps_free = [barrier["act"]]
            ups_free = [barrier["dve"]]
            hq_free = [dict(pe=0) for _ in range(2)]
            tp_copy = [barrier["act"], barrier["act"]]
            ffn_rd = [barrier["dve"], barrier["dve"]]
            ffn_free = [0, 0]
            hq_used = 0
            xqt_free = [0, 0]
            tq_cycle = 0
            nq2 = 0
            for sc in range(NSC):
                j = sc % 2
                for dc in range(NDC):
                    qeng, qsem = (("sp", "spq") if dc % 2 == 0 else ("act", "actq"))
                    for hh in range(2):
                        sch.emit(qeng, lambda e, dc=dc, sc=sc, hh=hh, j=j:
                                 e.dma_start_transpose(
                            xqt[j][dc][:, hh * 128:(hh + 1) * 128],
                            xq_hbm[sc * 256 + hh * 128:sc * 256 + (hh + 1) * 128,
                                   dc * 128:(dc + 1) * 128]), qsem, 16,
                            bw + [("pe", xqt_free[j])])
                xq_ld = (sch.n("spq"), sch.n("actq"))
                pend = []

                def flush_pend(qd_override=None):
                    nonlocal tq_cycle
                    if not pend:
                        return
                    jqf = pend[0][2]
                    for (ic, thf, jqp, q, qdp) in pend:
                        tj = tq_cycle % 2
                        tq_cycle += 1
                        qdw = qd_override if qd_override is not None else qdp
                        sch.emit("pe", lambda e, jqp=jqp, q=q, tj=tj: e.transpose(
                            tpps[tj][:], hq2[jqp][:, q * 128:(q + 1) * 128],
                            identb[:]),
                            "pe", 1, [("dve", qdw), ("act", tp_copy[tj])])
                        tn = sch.n("pe")
                        sch.emit("act", lambda e, ic=ic, thf=thf, tj=tj: e.copy(
                            hqT[ic][:, thf * 128:(thf + 1) * 128], tpps[tj][:]),
                            "act", 1, [("pe", tn)])
                        tp_copy[tj] = sch.n("act")
                    hq_free[jqf] = dict(pe=sch.n("pe"))
                    pend.clear()

                for th in range(2):
                    for ich, (ioff, W) in enumerate(ICH):
                        needw = wgq_loaded[(ioff + W - 1) // 128]
                        gw = [("spq", max(needw[0], xq_ld[0])),
                              ("actq", max(needw[1], xq_ld[1])),
                              ("act", gps_free[0])]
                        for dc in range(NDC):
                            sch.emit("pe", lambda e, dc=dc, ioff=ioff, W=W, th=th, j=j:
                                     e.matmul(
                                gps[:, 0:W], xqt[j][dc][:, th * 128:(th + 1) * 128],
                                WGQT[dc][:, ioff:ioff + W],
                                start=(dc == 0), stop=(dc == NDC - 1)), "pe", 1,
                                gw if dc == 0 else ())
                        gmm = sch.n("pe")
                        sch.emit("act", lambda e, W=W: e.activation(
                            sg[:, 0:W], gps[:, 0:W], AF.Silu),
                            "act", 1, [("pe", gmm), ("dve", silu_free[0])])
                        silu_done = sch.n("act")
                        gps_free[0] = silu_done
                        for dc in range(NDC):
                            sch.emit("pe", lambda e, dc=dc, ioff=ioff, W=W, th=th, j=j:
                                     e.matmul(
                                ups[:, 0:W], xqt[j][dc][:, th * 128:(th + 1) * 128],
                                WUQT[dc][:, ioff:ioff + W],
                                start=(dc == 0), stop=(dc == NDC - 1)), "pe", 1,
                                [("dve", ups_free[0])] if dc == 0 else ())
                        umm = sch.n("pe")
                        sch.emit("dve", lambda e, W=W: e.tensor_tensor(
                            hbuf[:, 0:W], sg[:, 0:W], ups[:, 0:W], ALU.mult),
                            "dve", 1, [("pe", umm), ("act", silu_done)])
                        silu_free[0] = sch.n("dve")
                        ups_free[0] = sch.n("dve")
                        hb = sch.n("dve")
                        jq = nq2 % 2
                        nq2 += 1
                        quant256(sch, hbuf[:, 0:W], hq2[jq][:, 0:W],
                                 [("dve", hb)],
                                 [("pe", hq_free[jq].get("pe", 0)),
                                  ("poolq", hq_free[jq].get("poolq", 0))],
                                 qtb[jq])
                        qd = sch.n("dve")
                        if dbg:
                            sch.emit("pool", lambda e, sc=sc, th=th, ioff=ioff,
                                     W=W, jq=jq: e.dma_start(
                                hq_hbm[sc * 256 + th * 128:sc * 256 + (th + 1) * 128,
                                       ioff:ioff + W], hq2[jq][:, 0:W]),
                                "poolq", 16, [("dve", qd)])
                            hq_free[jq]["poolq"] = sch.n("poolq")
                        flush_pend()
                        for q in range(W // 128):
                            pend.append((ioff // 128 + q, th, jq, q, qd))
                for _ in range(3):
                    sch.emit("dve", lambda e: e.tensor_copy(
                        settleB[:], hq2[0][:, 0:64]), "dve", 1)
                flush_pend(qd_override=sch.n("dve"))
                tp_done = sch.n("act")
                for sub in range(2):
                    dj = sub
                    dw = [("act", tp_done), ("dve", ffn_rd[dj])]
                    for ic in range(NIC):
                        for hh in range(2):
                            sch.emit("pe", lambda e, ic=ic, hh=hh, sub=sub, dj=dj: e.matmul(
                                dps[dj][:, hh * 512:(hh + 1) * 512],
                                hqT[ic][:, sub * 128:(sub + 1) * 128],
                                WDQT[ic][:, hh * 512:(hh + 1) * 512],
                                start=(ic == 0), stop=(ic == NIC - 1)), "pe", 1,
                                dw if (ic == 0 and hh == 0) else ())
                    dmm = sch.n("pe")
                    sch.emit("dve", lambda e, dj=dj: e.tensor_copy(ffnst[dj][:], dps[dj][:]),
                             "dve", 1, [("pe", dmm), ("poolq", ffn_free[dj])])
                    ffn_rd[dj] = sch.n("dve")
                    settle(sch, settleB[:], ffnst[dj][:, 0:64])
                    dvn = sch.n("dve")
                    sch.emit("pool", lambda e, sc=sc, sub=sub, dj=dj: e.dma_start(
                        ffn_hbm[sc * 256 + sub * 128:sc * 256 + (sub + 1) * 128, :],
                        ffnst[dj][:]), "poolq", 16, [("dve", dvn)])
                    ffn_free[dj] = sch.n("poolq")
                hq_used = sch.n("pe")
                xqt_free[j] = sch.n("pe")

            barrierB = {s: sch.n(s) for s in ALLSEMS}
            with nc.Block() as blockB:
                sch.run(blockB, sems)

        with ExitStack() as sc_:
            en = sc_.enter_context
            ow_bf = [en(nc.sbuf_tensor(f"owbf{d}", [128, D], BF16)) for d in range(NDC)]
            ow_st = [en(nc.sbuf_tensor(f"owst{j}", [128, D], F32)) for j in range(2)]
            aot_sl = [en(nc.sbuf_tensor(f"aotsl{j}", [128, D], BF16)) for j in range(2)]
            xc2 = [en(nc.sbuf_tensor(f"xc2_{j}", [128, D], F32)) for j in range(2)]
            ffnt = [en(nc.sbuf_tensor(f"ffnt{j}", [128, D], BF16)) for j in range(2)]
            obbc = en(nc.sbuf_tensor("obbc", [128, D], F32))
            obrow = en(nc.sbuf_tensor("obrow", [1, D], F32))
            outst = [en(nc.sbuf_tensor(f"outst{j}", [128, D], F32)) for j in range(2)]
            onec = en(nc.sbuf_tensor("onec", [1, 128], F32))
            settleC = en(nc.sbuf_tensor("settleC", [128, 64], F32))
            ops_ = [en(nc.psum_tensor(f"ops{j}", [128, D], F32)) for j in range(2)]

            bwc = [(s, barrierB[s]) for s in ALLSEMS]
            sch.emit("pool", lambda e: e.memset(onec[:], 1.0), "pool", 1, bwc)
            onecd = sch.n("pool")
            sch.emit("sp", lambda e: e.dma_start(obrow[:], ob_h.unsqueeze(0)), "spq", 16, bwc)
            obl = sch.n("spq")
            sch.emit("pe", lambda e: e.matmul(ops_[0][:, 0:512], onec[:], obrow[:, 0:512],
                                              start=True, stop=True), "pe", 1,
                     bwc + [("spq", obl), ("pool", onecd)])
            sch.emit("pe", lambda e: e.matmul(ops_[0][:, 512:1024], onec[:],
                                              obrow[:, 512:1024],
                                              start=True, stop=True), "pe", 1)
            pmm = sch.n("pe")
            sch.emit("dve", lambda e: e.tensor_copy(obbc[:], ops_[0][:]), "dve", 1,
                     bwc + [("pe", pmm)])
            obbcd = sch.n("dve")
            owc = []
            for dc in range(NDC):
                j = dc % 2
                wt = bwc + ([("dve", owc[dc - 2])] if dc >= 2 else [])
                sch.emit("sp", lambda e, dc=dc, j=j: e.dma_start(
                    ow_st[j][:], ow_h[dc * 128:(dc + 1) * 128, :]), "spq", 16, wt)
                ld = sch.n("spq")
                sch.emit("dve", lambda e, dc=dc, j=j: e.tensor_copy(ow_bf[dc][:],
                                                                    ow_st[j][:]),
                         "dve", 1, [("spq", ld)])
                owc.append(sch.n("dve"))
            ow_done = sch.n("dve")

            ld_free = [(0, 0), (0, 0)]
            out_free = [0, 0]
            out_rd = [obbcd, 0]
            for st in range(ST):
                j = st % 2
                for dc in range(NDC):
                    qeng, qsem = (("sp", "spq") if dc % 2 == 0 else ("act", "actq"))
                    sch.emit(qeng, lambda e, st=st, dc=dc, j=j: e.dma_start(
                        aot_sl[j][:, dc * 128:(dc + 1) * 128],
                        aot_hbm[dc * 128:(dc + 1) * 128, st * 128:(st + 1) * 128]),
                        qsem, 16, bwc + [("pe", ld_free[j][0])])
                sch.emit("sp", lambda e, st=st, j=j: e.dma_start(
                    xc2[j][:], x_h[st * 128:(st + 1) * 128, :]), "spq", 16,
                    bwc + [("dve", ld_free[j][1])])
                sch.emit("act", lambda e, st=st, j=j: e.dma_start(
                    ffnt[j][:], ffn_hbm[st * 128:(st + 1) * 128, :]), "actq", 16,
                    bwc + [("dve", ld_free[j][1])])
                lded = (sch.n("spq"), sch.n("actq"))
                mw = [("spq", lded[0]), ("actq", lded[1]), ("dve", ow_done),
                      ("dve", out_rd[j])]
                for dc in range(NDC):
                    for hh in range(2):
                        sch.emit("pe", lambda e, dc=dc, hh=hh, j=j: e.matmul(
                            ops_[j][:, hh * 512:(hh + 1) * 512],
                            aot_sl[j][:, dc * 128:(dc + 1) * 128],
                            ow_bf[dc][:, hh * 512:(hh + 1) * 512],
                            start=(dc == 0), stop=(dc == NDC - 1)), "pe", 1,
                            mw if (dc == 0 and hh == 0) else ())
                omm = sch.n("pe")
                sch.emit("dve", lambda e, j=j: e.tensor_tensor(
                    outst[j][:], ops_[j][:], xc2[j][:], ALU.add), "dve", 1,
                    [("pe", omm), ("actq", out_free[j])])
                sch.emit("dve", lambda e, j=j: e.tensor_tensor(
                    outst[j][:], outst[j][:], ffnt[j][:], ALU.add), "dve", 1)
                sch.emit("dve", lambda e, j=j: e.tensor_tensor(
                    outst[j][:], outst[j][:], obbc[:], ALU.add), "dve", 1)
                ld_free[j] = (omm, sch.n("dve"))
                out_rd[j] = sch.n("dve")
                settle(sch, settleC[:], outst[j][:, 0:64])
                wrd = sch.n("dve")
                sch.emit("act", lambda e, st=st, j=j: e.dma_start(
                    out_h[st * 128:(st + 1) * 128, :], outst[j][:]), "actq", 16,
                    [("dve", wrd)])
                out_free[j] = sch.n("actq")
            fin = sch.n("actq")
            sch.emit("pool", lambda e: e.engine_nop(), None, 0, [("actq", fin)])
            with nc.Block() as blockC:
                sch.run(blockC, sems)

    return nc

from concourse.bass_utils import run_bass_kernel_spmd

_NC_CACHE = {}


def kernel(L0_attn_x, L0_attn_q_w, L0_attn_q_b, L0_attn_k_w, L0_attn_k_b,
           L0_attn_v_w, L0_attn_v_b, L0_attn_o_w, L0_attn_o_b,
           L0_ffn_w_gate, L0_ffn_w_up, L0_ffn_w_down):
    x = np.ascontiguousarray(np.asarray(L0_attn_x, dtype=np.float32))
    B, S, Dm = x.shape
    assert (S, Dm) == (4096, 1024) and B == 8
    if "nc" not in _NC_CACHE:
        _NC_CACHE["nc"] = build_program(S=S)
    nc = _NC_CACHE["nc"]
    w = dict(
        qw=np.ascontiguousarray(L0_attn_q_w, dtype=np.float32),
        qb=np.ascontiguousarray(L0_attn_q_b, dtype=np.float32),
        kw=np.ascontiguousarray(L0_attn_k_w, dtype=np.float32),
        kb=np.ascontiguousarray(L0_attn_k_b, dtype=np.float32),
        vw=np.ascontiguousarray(L0_attn_v_w, dtype=np.float32),
        vb=np.ascontiguousarray(L0_attn_v_b, dtype=np.float32),
        ow=np.ascontiguousarray(L0_attn_o_w, dtype=np.float32),
        ob=np.ascontiguousarray(L0_attn_o_b, dtype=np.float32),
        wg=np.ascontiguousarray(L0_ffn_w_gate, dtype=np.float32),
        wu=np.ascontiguousarray(L0_ffn_w_up, dtype=np.float32),
        wd=np.ascontiguousarray(L0_ffn_w_down, dtype=np.float32),
    )
    import ml_dtypes
    ident = np.eye(128, dtype=ml_dtypes.bfloat16)
    in_maps = [dict(x=np.ascontiguousarray(x[c]), ident=ident, **w) for c in range(B)]
    res = run_bass_kernel_spmd(nc, in_maps, core_ids=list(range(B)))
    return np.stack([np.asarray(res.results[c]["out"], dtype=np.float32)
                     for c in range(B)], axis=0)
